# version 11 (final)
# speedup vs baseline: 1.6307x; 1.0079x over previous
import os
os.environ.setdefault("GAUGE_NTFF_IGNORE_DMA", "1")

import numpy as np
import ml_dtypes
from contextlib import ExitStack

import concourse.bass as bass
import concourse.mybir as mybir
from concourse.bass_utils import run_bass_kernel_spmd

F32 = mybir.dt.float32
BF16 = mybir.dt.bfloat16
AF = mybir.ActivationFunctionType
ALU = mybir.AluOpType
AX = mybir.AxisListType

P = 128
DM = 1024
DH = 4096
NFC = DM // P
NHC = DH // P
TC = 512
LN_EPS = 1e-5
L2_EPS = 1e-12


class Sem:
    __slots__ = ("h", "n")

    def __init__(self, h):
        self.h = h
        self.n = 0

    def inc(self, k=1):
        self.n += k
        return self.n


class SharedState(dict):

    def __init__(self, em):
        super().__init__()
        self.em = em
        self._sems = {}

    def sem(self, name):
        if name not in self._sems:
            self._sems[name] = self.em.sem(name)
        return self._sems[name]

    def all_sem_tokens(self):
        return [(s, s.n) for s in self._sems.values()]


class Em:
    ENGINES = ("sync", "tensor", "scalar", "vector", "gpsimd")

    def __init__(self, nc, ctx):
        self.nc = nc
        self.ctx = ctx
        self.q = {e: [] for e in self.ENGINES}

    def sem(self, name):
        return Sem(self.ctx.enter_context(self.nc.semaphore(name)))

    def op(self, eng, build, waits=(), inc=None, inck=1):
        waits = tuple((s.h, v) for t in waits if t is not None for (s, v) in (t,))
        tok = None
        if inc is not None:
            v = inc.inc(inck)
            tok = (inc, v)

            def f(e, build=build, waits=waits, ih=inc.h, iv=inck):
                for (h, v_) in waits:
                    e.wait_ge(h, v_)
                build(e).then_inc(ih, iv)
        else:
            def f(e, build=build, waits=waits):
                for (h, v_) in waits:
                    e.wait_ge(h, v_)
                build(e)
        self.q[eng].append(f)
        return tok

    def wait(self, eng, waits):
        waits = tuple((s.h, v) for t in waits if t is not None for (s, v) in (t,))

        def f(e, waits=waits):
            for (h, v_) in waits:
                e.wait_ge(h, v_)
        self.q[eng].append(f)

    def finish(self):
        block = self.ctx.enter_context(self.nc.Block())
        for name in self.ENGINES:
            ops = self.q[name]
            if ops:
                getattr(block, name)(lambda e, ops=ops: [f(e) for f in ops])


def _setup_consts(em, ctx, nc):
    c = {}
    c["ones_col"] = ctx.enter_context(nc.sbuf_tensor("ones_col", [P, 1], F32))
    c["ones_row"] = ctx.enter_context(nc.sbuf_tensor("ones_row", [1, P], F32))
    c["id32"] = ctx.enter_context(nc.sbuf_tensor("id32", [P, P], F32))
    c["eps"] = ctx.enter_context(nc.sbuf_tensor("eps_c", [P, 1], F32))
    c["eps24"] = ctx.enter_context(nc.sbuf_tensor("eps24_c", [P, 1], F32))
    s_pool = em.sem("s_pool_const")
    em.op("gpsimd", lambda e: e.memset(c["ones_col"][:], 1.0), inc=s_pool)
    em.op("gpsimd", lambda e: e.memset(c["ones_row"][:], 1.0), inc=s_pool)
    em.op("gpsimd", lambda e: e.memset(c["eps"][:], LN_EPS), inc=s_pool)
    em.op("gpsimd", lambda e: e.memset(c["eps24"][:], L2_EPS * L2_EPS), inc=s_pool)
    tz = em.op("gpsimd", lambda e: e.memset(c["id32"][:], 0.0), inc=s_pool)
    em.op("gpsimd", lambda e: e.affine_select(
        out=c["id32"][:], in_=c["id32"][:], compare_op=ALU.not_equal,
        fill=1.0, base=0, pattern=[[-1, P]], channel_multiplier=1),
        waits=[tz], inc=s_pool)
    c["id_bf"] = ctx.enter_context(nc.sbuf_tensor("id_bf_c", [P, P], BF16))
    tzb = em.op("gpsimd", lambda e: e.memset(c["id_bf"][:], 0.0), inc=s_pool)
    tf = em.op("gpsimd", lambda e: e.affine_select(
        out=c["id_bf"][:], in_=c["id_bf"][:], compare_op=ALU.not_equal,
        fill=1.0, base=0, pattern=[[-1, P]], channel_multiplier=1),
        waits=[tzb], inc=s_pool)
    c["tok"] = tf
    return c


def emit_ln(em, st, x_sb, xtok, y_bf, war_y=None, war_banks=None, stats_cb=None,
            n_fc=NFC, tc=TC, y_dest=None):
    pS0, pS1 = st["pBC0"], st["pBC1"]
    war_banks = war_banks or []
    war_y = war_y or []

    last_sum = None
    for c in range(n_fc):
        w = [xtok] + (war_banks + [st["const_tok"]] if c == 0 else [])
        last_sum = em.op("tensor", lambda e, c=c: e.matmul(
            pS0[0:1, :], st["ones_col"][:], x_sb[:, c * tc:(c + 1) * tc],
            start=(c == 0), stop=(c == n_fc - 1)), waits=w, inc=st["m_pe"])
    last_sq = None
    pend_sq = None

    def emit_sq(c, s, tsq):
        return em.op("tensor", lambda e, c=c, s=s: e.matmul(
            pS1[0:1, :], st["ones_col"][:], st["xsq"][:, s * tc:(s + 1) * tc],
            start=(c == 0), stop=(c == n_fc - 1)),
            waits=[tsq] + (war_banks if c == 0 else []), inc=st["m_pe"])

    for c in range(n_fc):
        s = c % 2
        tsq = em.op("scalar", lambda e, c=c, s=s: e.activation(
            st["xsq"][:, s * tc:(s + 1) * tc], x_sb[:, c * tc:(c + 1) * tc],
            AF.Square), waits=[xtok, st["xsq_free"][s]], inc=st["m_act"])
        if pend_sq is not None:
            st["xsq_free"][pend_sq[1]] = emit_sq(*pend_sq)
        pend_sq = (c, s, tsq)
    last_sq = emit_sq(*pend_sq)
    st["xsq_free"][pend_sq[1]] = last_sq

    rows = st["rows"]
    def row(i):
        return rows[0:1, i * tc:(i + 1) * tc]
    t_mean = em.op("scalar", lambda e: e.activation(
        row(0), pS0[0:1, :], AF.Copy, scale=1.0 / DM),
        waits=[last_sum, st["rows_free"]], inc=st["m_act"])
    t_msq = em.op("scalar", lambda e: e.activation(
        row(1), pS0[0:1, :], AF.Square, scale=1.0 / DM),
        waits=[last_sum], inc=st["m_act"])
    t_var = em.op("vector", lambda e: e.scalar_tensor_tensor(
        row(2), pS1[0:1, :], 1.0 / DM, row(1), ALU.mult, ALU.subtract),
        waits=[last_sq, t_msq, st["rows_free"]], inc=st["m_dve"])
    t_sqr = em.op("scalar", lambda e: e.activation(
        row(3), row(2), AF.Ln, bias=st["eps"][0:1, 0:1]),
        waits=[t_var, st["const_tok"]], inc=st["m_act"])
    t_rstd = em.op("scalar", lambda e: e.activation(
        row(4), row(3), AF.Exp, scale=-0.5),
        waits=[t_sqr, st["rows_free"]], inc=st["m_act"])
    extra = stats_cb(t_var, t_rstd, row) if stats_cb is not None else None
    t_negmr = em.op("vector", lambda e: e.scalar_tensor_tensor(
        row(5), row(0), -1.0, row(4), ALU.mult, ALU.mult),
        waits=[t_mean, t_rstd], inc=st["m_dve"])
    t_arep = em.op("tensor", lambda e: e.matmul(
        pS0[:, :], st["ones_row"][:], row(4), start=True, stop=True),
        waits=[t_rstd, t_msq], inc=st["m_pe"])
    t_brep = em.op("tensor", lambda e: e.matmul(
        pS1[:, :], st["ones_row"][:], row(5), start=True, stop=True),
        waits=[t_negmr] + ([extra] if extra else []), inc=st["m_pe"])

    if y_dest is None:
        y_dest = lambda c: y_bf[:, c * tc:(c + 1) * tc]
    last_y = None
    for c in range(n_fc):
        s = c % 2
        t_t1 = em.op("vector", lambda e, c=c, s=s: e.tensor_tensor(
            st["txy"][:, s * tc:(s + 1) * tc], x_sb[:, c * tc:(c + 1) * tc],
            pS0[:, :], ALU.mult),
            waits=[xtok, t_arep, st["txy_free"][s]], inc=st["m_dve"])
        last_y = em.op("vector", lambda e, c=c, s=s: e.tensor_tensor(
            y_dest(c), st["txy"][:, s * tc:(s + 1) * tc],
            pS1[:, :], ALU.add),
            waits=[t_t1, t_brep] + (war_y if c == 0 else []), inc=st["m_dve"])
        st["txy_free"][s] = last_y
    st["rows_free"] = t_negmr if extra is None or extra[1] < t_negmr[1] else extra
    st["bank_free_BC"] = last_y
    return last_y, [last_sum, last_sq, last_y]


def emit_ffn(em, st, io, n_tok, tag):
    nc = em.nc
    xT, yT = io["xT"], io["yT"]
    w1T, w2T, h1T, h2T, w3T = io["w1T"], io["w2T"], io["h1T"], io["h2T"], io["w3T"]
    s1, s2, sh1, sh2, s3 = io["s1"], io["s2"], io["sh1"], io["sh2"], io["s3"]
    nch = n_tok // TC
    with ExitStack() as ctx:
        e_ = lambda cm: ctx.enter_context(cm)
        x_sb = [e_(nc.sbuf_tensor(f"x_sb{i}_{tag}", [P, NFC * TC], F32)) for i in range(2)]
        y_bf = e_(nc.sbuf_tensor(f"y_bf_{tag}", [P, NFC * TC], BF16))
        hX = e_(nc.sbuf_tensor(f"hX_{tag}", [P, NHC * TC], BF16))
        hY = e_(nc.sbuf_tensor(f"hY_{tag}", [P, NHC * TC], BF16))
        w1b = [e_(nc.sbuf_tensor(f"w1b{i}_{tag}", [P, NFC * 512], BF16)) for i in range(2)]
        w2b = [e_(nc.sbuf_tensor(f"w2b{i}_{tag}", [P, NFC * 512], BF16)) for i in range(2)]
        hb = [e_(nc.sbuf_tensor(f"hb{i}_{tag}", [P, NHC * P], BF16)) for i in range(2)]
        w3b = [e_(nc.sbuf_tensor(f"w3b{i}_{tag}", [P, NHC * P], BF16)) for i in range(2)]
        asil = e_(nc.sbuf_tensor(f"asil_{tag}", [P, 2 * TC], F32))
        out_sb = e_(nc.sbuf_tensor(f"out_sb_{tag}", [P, 2 * TC], F32))
        scs = e_(nc.sbuf_tensor(f"scs_{tag}", [P, 4 * NHC + NFC], F32))
        scraw = e_(nc.sbuf_tensor(f"scraw_{tag}", [NHC, P], F32))
        pA = [e_(nc.psum_tensor(f"pA{i}_{tag}", [P, TC], F32)) for i in range(2)]
        pB = [e_(nc.psum_tensor(f"pB{i}_{tag}", [P, TC], F32)) for i in range(2)]
        pH = [e_(nc.psum_tensor(f"pH{i}_{tag}", [P, TC], F32)) for i in range(2)]

        d_x = [st.sem("d_x0"), st.sem("d_x1")]
        d_sc = st.sem("d_sc")
        d_out = [st.sem("d_out0"), st.sem("d_out1")]
        d_w1 = [st.sem("d_w1a"), st.sem("d_w1b")]
        d_w2 = [st.sem("d_w2a"), st.sem("d_w2b")]
        d_h = [st.sem("d_ha"), st.sem("d_hb")]
        d_w3 = [st.sem("d_w3a"), st.sem("d_w3b")]
        consts = {"tok": st["const_tok"]}
        id32 = st["id32"]

        sc_last = None
        prev_read = None
        for (vec, col0, n_) in [(s1, 0, NHC), (s2, NHC, NHC), (sh1, 2 * NHC, NHC),
                                (sh2, 3 * NHC, NHC), (s3, 4 * NHC, NFC)]:
            mul = 0.5 if vec is s3 else 1.0
            tl = em.op("sync", lambda e, vec=vec, n=n_: e.dma_start(
                scraw[0:n, :], vec.rearrange("(c p) -> c p", p=P)),
                waits=[prev_read], inc=d_sc, inck=16)
            ttr = em.op("tensor", lambda e, n=n_: e.transpose(
                st["pBC0"][:, 0:n], scraw[0:n, :], id32[0:n, 0:n]),
                waits=[tl, consts["tok"], prev_read], inc=st["m_pe"])
            tcp = em.op("vector", lambda e, col0=col0, n=n_, mul=mul: e.tensor_scalar(
                scs[:, col0:col0 + n], st["pBC0"][:, 0:n], mul, None, ALU.mult),
                waits=[ttr], inc=st["m_dve"])
            prev_read = tcp
            sc_last = tcp

        def scol(group, m):
            off = {"s1": 0, "s2": NHC, "sh1": 2 * NHC, "sh2": 3 * NHC,
                   "s3h": 4 * NHC}[group]
            return scs[:, off + m:off + m + 1]

        w1_free = [None, None]; w2_free = [None, None]
        h_free = [None, None]; w3_free = [None, None]
        x_free = [None, None]
        asil_free = [None, None]
        out_free = [None, None]
        st["xsq_free"] = [None, None]
        st["txy_free"] = [None, None]
        st["rows_free"] = None
        hX_war = None
        hY_war = None
        y_war = None
        war_banks = [sc_last]
        total_stores = 0

        gates = io.get("gates") or {}
        g_12 = gates.get("g12", [])
        g_h1 = gates.get("h1", [])
        g_h2 = gates.get("h2", [])
        g_w3 = gates.get("w3", [])

        def dma_w1(g, slot, wait, gate=()):
            return em.op("scalar", lambda e, g=g, slot=slot: e.dma_start(
                w1b[slot][:], w1T[g, :, :]),
                waits=[wait] + list(gate), inc=d_w1[slot], inck=16)

        def dma_w2(g, slot, wait, gate=()):
            return em.op("scalar", lambda e, g=g, slot=slot: e.dma_start(
                w2b[slot][:], w2T[g, :, :]),
                waits=[wait] + list(gate), inc=d_w2[slot], inck=16)

        def dma_h(src, m, slot, wait, gate=()):
            return em.op("sync" if slot == 0 else "gpsimd",
                         lambda e, src=src, m=m, slot=slot: e.dma_start(
                hb[slot][:], src[m, :, :]),
                waits=[wait] + list(gate), inc=d_h[slot], inck=16)

        def dma_w3(m, slot, wait, gate=()):
            return em.op("scalar" if slot == 0 else "gpsimd",
                         lambda e, m=m, slot=slot: e.dma_start(
                w3b[slot][:], w3T[m, :, :]),
                waits=[wait] + list(gate), inc=d_w3[slot], inck=16)

        for ci in range(nch):
            xs = ci % 2
            t_x = em.op("sync", lambda e, ci=ci, xs=xs: e.dma_start(
                x_sb[xs][:].rearrange("p (c t) -> p c t", t=TC),
                xT[:, ci * TC:(ci + 1) * TC].rearrange("(c p) t -> p c t", p=P)),
                waits=[x_free[xs]], inc=d_x[xs], inck=16)

            y_all, _ = emit_ln(em, st, x_sb[xs], t_x, y_bf,
                               war_y=[y_war], war_banks=war_banks)
            war_banks = [st["bank_free_BC"]]

            c0 = (ci == 0)
            w1_toks = [dma_w1(0, 0, w1_free[0], g_12 if c0 else ()),
                       dma_w1(1, 1, w1_free[1], g_12 if c0 else ())]
            w2_toks = [dma_w2(0, 0, w2_free[0], g_12 if c0 else ()),
                       dma_w2(1, 1, w2_free[1], g_12 if c0 else ())]

            prod_last = None
            pA_free = [None, None]; pB_free = [None, None]
            w1_last_use = [None, None]; w2_last_use = [None, None]
            for m in range(NHC):
                g, gs, ms = m // 4, (m // 4) % 2, m % 2
                j = m % 4
                mm1 = None
                for k in range(NFC):
                    mm1 = em.op("tensor", lambda e, k=k, gs=gs, j=j, ms=ms: e.matmul(
                        pA[ms][:, :], w1b[gs][:, k * 512 + j * P:k * 512 + (j + 1) * P],
                        y_bf[:, k * TC:(k + 1) * TC],
                        start=(k == 0), stop=(k == NFC - 1)),
                        waits=([w1_toks[g], y_all, pA_free[ms]] if k == 0 else []),
                        inc=(st["m_pe"] if k == NFC - 1 else None))
                w1_last_use[gs] = mm1
                mm2 = None
                for k in range(NFC):
                    mm2 = em.op("tensor", lambda e, k=k, gs=gs, j=j, ms=ms: e.matmul(
                        pB[ms][:, :], w2b[gs][:, k * 512 + j * P:k * 512 + (j + 1) * P],
                        y_bf[:, k * TC:(k + 1) * TC],
                        start=(k == 0), stop=(k == NFC - 1)),
                        waits=([w2_toks[g], y_all, pB_free[ms]] if k == 0 else []),
                        inc=(st["m_pe"] if k == NFC - 1 else None))
                w2_last_use[gs] = mm2
                if j == 3 and g + 2 < 8:
                    w1_toks.append(dma_w1(g + 2, gs, mm1))
                    w2_toks.append(dma_w2(g + 2, gs, mm2))
                ta = em.op("scalar", lambda e, m=m, ms=ms: e.activation(
                    asil[:, ms * TC:(ms + 1) * TC], pA[ms][:, :], AF.Silu,
                    scale=scol("s1", m)),
                    waits=[mm1, sc_last, asil_free[ms]], inc=st["m_act"])
                pA_free[ms] = ta
                tp = em.op("vector", lambda e, m=m, ms=ms: e.scalar_tensor_tensor(
                    hX[:, m * TC:(m + 1) * TC], pB[ms][:, :], scol("s2", m),
                    asil[:, ms * TC:(ms + 1) * TC], ALU.mult, ALU.mult),
                    waits=[mm2, sc_last, ta, hX_war], inc=st["m_dve"])
                pB_free[ms] = tp
                asil_free[ms] = tp
                prod_last = tp
            y_war = w2_last_use[1] or w2_last_use[0]
            w1_free = [w1_last_use[0], w1_last_use[1]]
            w2_free = [w2_last_use[0], w2_last_use[1]]

            h_toks = [dma_h(h1T, 0, 0, h_free[0], g_h1 if c0 else ()),
                      dma_h(h1T, 1, 1, h_free[1], g_h1 if c0 else ())]
            pH_free = [None, None]
            gel_last = [None, None]
            h1_mm_last = [None, None]
            for m in range(NHC):
                hs = m % 2
                t1 = None
                for k in range(NHC):
                    t1 = em.op("tensor", lambda e, k=k, hs=hs: e.matmul(
                        pH[hs][:, :], hb[hs][:, k * P:(k + 1) * P],
                        hX[:, k * TC:(k + 1) * TC],
                        start=(k == 0), stop=(k == NHC - 1)),
                        waits=([h_toks[m], prod_last, pH_free[hs]] if k == 0 else []),
                        inc=(st["m_pe"] if k == NHC - 1 else None))
                h1_mm_last[hs] = t1
                if m + 2 < NHC:
                    h_toks.append(dma_h(h1T, m + 2, hs, t1, g_h1 if c0 else ()))
                else:
                    h_toks.append(dma_h(h2T, m + 2 - NHC, hs, t1, g_h2 if c0 else ()))
                tg = em.op("scalar", lambda e, m=m, hs=hs: e.activation(
                    hY[:, m * TC:(m + 1) * TC], pH[hs][:, :], AF.Gelu,
                    scale=scol("sh1", m)),
                    waits=[t1, sc_last, hY_war, gel_last[hs]], inc=st["m_act"])
                pH_free[hs] = tg
                gel_last[hs] = tg
            gel1_all = gel_last[1] or gel_last[0]
            hX_war_mid = h1_mm_last[1]

            gel_last = [None, None]
            h2_mm_last = [None, None]
            for m in range(NHC):
                hs = m % 2
                t1 = None
                for k in range(NHC):
                    t1 = em.op("tensor", lambda e, k=k, hs=hs: e.matmul(
                        pH[hs][:, :], hb[hs][:, k * P:(k + 1) * P],
                        hY[:, k * TC:(k + 1) * TC],
                        start=(k == 0), stop=(k == NHC - 1)),
                        waits=([h_toks[NHC + m], gel1_all, pH_free[hs]] if k == 0 else []),
                        inc=(st["m_pe"] if k == NHC - 1 else None))
                h2_mm_last[hs] = t1
                h_free[hs] = t1
                if m + 2 < NHC:
                    h_toks.append(dma_h(h2T, m + 2, hs, t1, g_h2 if c0 else ()))
                tg = em.op("scalar", lambda e, m=m, hs=hs: e.activation(
                    hX[:, m * TC:(m + 1) * TC], pH[hs][:, :], AF.Gelu,
                    scale=scol("sh2", m)),
                    waits=[t1, sc_last, hX_war_mid, gel_last[hs]], inc=st["m_act"])
                pH_free[hs] = tg
                gel_last[hs] = tg
            gel2_all = gel_last[1] or gel_last[0]
            hY_war = h2_mm_last[1]

            w3_toks = [dma_w3(0, 0, w3_free[0], g_w3 if c0 else ()),
                       dma_w3(1, 1, w3_free[1], g_w3 if c0 else ())]
            res_last = None
            for m in range(NFC):
                ws = m % 2
                t1 = None
                for k in range(NHC):
                    t1 = em.op("tensor", lambda e, k=k, ws=ws: e.matmul(
                        pH[ws][:, :], w3b[ws][:, k * P:(k + 1) * P],
                        hX[:, k * TC:(k + 1) * TC],
                        start=(k == 0), stop=(k == NHC - 1)),
                        waits=([w3_toks[m], gel2_all, pH_free[ws]] if k == 0 else []),
                        inc=(st["m_pe"] if k == NHC - 1 else None))
                w3_free[ws] = t1
                if m + 2 < NFC:
                    w3_toks.append(dma_w3(m + 2, ws, t1, g_w3 if c0 else ()))
                tr_ = em.op("vector", lambda e, m=m, ws=ws, xs=xs: e.scalar_tensor_tensor(
                    out_sb[:, ws * TC:(ws + 1) * TC], pH[ws][:, :], scol("s3h", m),
                    x_sb[xs][:, m * TC:(m + 1) * TC], ALU.mult, ALU.add),
                    waits=[t1, sc_last, t_x, out_free[ws]], inc=st["m_dve"])
                pH_free[ws] = tr_
                ts_ = em.op("sync", lambda e, m=m, ws=ws, ci=ci: e.dma_start(
                    yT[m * P:(m + 1) * P, ci * TC:(ci + 1) * TC],
                    out_sb[:, ws * TC:(ws + 1) * TC]),
                    waits=[tr_], inc=d_out[ws], inck=16)
                out_free[ws] = ts_
                total_stores += 1
                res_last = tr_
            x_free[xs] = res_last
            hX_war = w3_free[1]

    return [(d_out[0], d_out[0].n), (d_out[1], d_out[1].n)]



def _t(a):
    return np.ascontiguousarray(np.asarray(a).T)


def _mk_shared(em, ctx, nc):
    st = SharedState(em)
    st["m_pe"] = st.sem("m_pe")
    st["m_act"] = st.sem("m_act")
    st["m_dve"] = st.sem("m_dve")
    consts = _setup_consts(em, ctx, nc)
    st["ones_col"], st["ones_row"] = consts["ones_col"], consts["ones_row"]
    st["eps"] = consts["eps"]
    st["eps24"] = consts["eps24"]
    st["const_tok"] = consts["tok"]
    st["id32"] = consts["id32"]
    st["id_bf"] = consts["id_bf"]
    st["id_bf_tok"] = consts["tok"]
    st["xsq"] = ctx.enter_context(nc.sbuf_tensor("sh_xsq", [P, 2 * TC], F32))
    st["txy"] = ctx.enter_context(nc.sbuf_tensor("sh_txy", [P, 2 * TC], F32))
    st["rows"] = ctx.enter_context(nc.sbuf_tensor("sh_rows", [1, 6 * TC], F32))
    st["pBC0"] = ctx.enter_context(nc.psum_tensor("sh_pBC0", [P, TC], F32))
    st["pBC1"] = ctx.enter_context(nc.psum_tensor("sh_pBC1", [P, TC], F32))
    st["xsq_free"] = [None, None]
    st["txy_free"] = [None, None]
    st["rows_free"] = None
    st["bank_free_BC"] = None
    return st


def _ffn_dram(nc, L):
    d = {}
    d["w1T"] = nc.dram_tensor(f"w1T_{L}", [DM, DH], F32, kind="ExternalInput").ap()
    d["w2T"] = nc.dram_tensor(f"w2T_{L}", [DM, DH], F32, kind="ExternalInput").ap()
    d["h1T"] = nc.dram_tensor(f"h1T_{L}", [DH, DH], F32, kind="ExternalInput").ap()
    d["h2T"] = nc.dram_tensor(f"h2T_{L}", [DH, DH], F32, kind="ExternalInput").ap()
    d["w3T"] = nc.dram_tensor(f"w3T_{L}", [DH, DM], F32, kind="ExternalInput").ap()
    d["s1"] = nc.dram_tensor(f"s1_{L}", [DH], F32, kind="ExternalInput").ap()
    d["s2"] = nc.dram_tensor(f"s2_{L}", [DH], F32, kind="ExternalInput").ap()
    d["sh1"] = nc.dram_tensor(f"sh1_{L}", [DH], F32, kind="ExternalInput").ap()
    d["sh2"] = nc.dram_tensor(f"sh2_{L}", [DH], F32, kind="ExternalInput").ap()
    d["s3"] = nc.dram_tensor(f"s3_{L}", [DM], F32, kind="ExternalInput").ap()
    return d


def _ffn_bf16(nc, L):
    d = {}
    d["w1c"] = nc.dram_tensor(f"w1c_{L}", [8, P, NFC * 512], BF16, kind="Internal").ap()
    d["w2c"] = nc.dram_tensor(f"w2c_{L}", [8, P, NFC * 512], BF16, kind="Internal").ap()
    d["h1c"] = nc.dram_tensor(f"h1c_{L}", [NHC, P, NHC * P], BF16, kind="Internal").ap()
    d["h2c"] = nc.dram_tensor(f"h2c_{L}", [NHC, P, NHC * P], BF16, kind="Internal").ap()
    d["w3c"] = nc.dram_tensor(f"w3c_{L}", [NFC, P, NHC * P], BF16, kind="Internal").ap()
    return d


class Preconv:

    def __init__(self, em):
        self.em = em
        self.gi = 0

    def group(self, name):
        return self.em.sem(f"d_cv_{name}")

    def block(self, sem, dst, src, m):
        return self.em.op("gpsimd", lambda e, dst=dst, src=src, m=m: e.dma_start(
            dst.rearrange("p (k m) -> p k m", m=m), src), inc=sem, inck=16)

    def grouped(self, sem, dst3, src2, K, M, ncol_blk):
        tok = None
        for g in range(M // ncol_blk):
            tok = self.block(sem, dst3[g, :, :],
                             src2[:, g * ncol_blk:(g + 1) * ncol_blk]
                             .rearrange("(k p) m -> p k m", p=P), ncol_blk)
        return tok


def lazy_cvt(cv, w, c, sem):
    def gen():
        for (dst, src, Kd, M, ncol) in [
                (c["w1c"], w["w1T"], DM, DH, 512),
                (c["w2c"], w["w2T"], DM, DH, 512),
                (c["h1c"], w["h1T"], DH, DH, P),
                (c["h2c"], w["h2T"], DH, DH, P),
                (c["w3c"], w["w3T"], DH, DM, P)]:
            for g in range(M // ncol):
                yield (dst[g, :, :],
                       src[:, g * ncol:(g + 1) * ncol]
                       .rearrange("(k p) m -> p k m", p=P), ncol)
    it = gen()
    last = [None]

    def step(n):
        for _ in range(n):
            item = next(it, None)
            if item is None:
                return last[0]
            last[0] = cv.block(sem, *item)
        return last[0]
    return step


def build_fused(n_tok):
    nc = bass.Bass("TRN2", target_bir_lowering=False)
    xin = nc.dram_tensor("x_in", [DM, n_tok], F32, kind="ExternalInput").ap()
    yout = nc.dram_tensor("y_out", [DM, n_tok], F32, kind="ExternalOutput").ap()
    xa = nc.dram_tensor("x_a", [DM, n_tok], F32, kind="Internal").ap()
    xb = nc.dram_tensor("x_b", [DM, n_tok], F32, kind="Internal").ap()
    wL0 = _ffn_dram(nc, "L0")
    wL1 = _ffn_dram(nc, "L1")
    cL0 = _ffn_bf16(nc, "L0")
    cL1 = _ffn_bf16(nc, "L1")
    qkvT_f = nc.dram_tensor("qkvT", [DM, 3 * DM], F32, kind="ExternalInput").ap()
    projT_f = nc.dram_tensor("projT", [DM, DM], F32, kind="ExternalInput").ap()
    qkvc = nc.dram_tensor("qkvc", [6, P, NFC * 512], BF16, kind="Internal").ap()
    projc = nc.dram_tensor("projc", [NFC, P, NFC * P], BF16, kind="Internal").ap()
    at = dict(
        qkvT=qkvc,
        projT=projc,
        pb=nc.dram_tensor("pb", [DM], F32, kind="ExternalInput").ap(),
        t16=nc.dram_tensor("t16", [16], F32, kind="ExternalInput").ap(),
        trep=nc.dram_tensor("trep", [DM], F32, kind="ExternalInput").ap(),
        qsc=nc.dram_tensor("qsc", [DM, n_tok], BF16, kind="Internal").ap(),
        vsc=nc.dram_tensor("vsc", [16, n_tok, 65], BF16, kind="Internal").ap(),
    )

    def ffn_io(w, c):
        return dict(w1T=c["w1c"], w2T=c["w2c"], h1T=c["h1c"], h2T=c["h2c"],
                    w3T=c["w3c"], s1=w["s1"], s2=w["s2"], sh1=w["sh1"],
                    sh2=w["sh2"], s3=w["s3"])

    with ExitStack() as ctx:
        em = Em(nc, ctx)
        st = _mk_shared(em, ctx, nc)
        cv = Preconv(em)

        def barrier():
            toks = st.all_sem_tokens()
            for eng in Em.ENGINES:
                em.wait(eng, toks)

        def cvt_layer(w, c, tagL):
            g = {}
            s12 = cv.group(f"w12_{tagL}")
            cv.grouped(s12, c["w1c"], w["w1T"], DM, DH, 512)
            g["g12"] = [cv.grouped(s12, c["w2c"], w["w2T"], DM, DH, 512)]
            sh1 = cv.group(f"h1_{tagL}")
            g["h1"] = [cv.grouped(sh1, c["h1c"], w["h1T"], DH, DH, P)]
            sh2 = cv.group(f"h2_{tagL}")
            g["h2"] = [cv.grouped(sh2, c["h2c"], w["h2T"], DH, DH, P)]
            sw3 = cv.group(f"w3_{tagL}")
            g["w3"] = [cv.grouped(sw3, c["w3c"], w["w3T"], DH, DM, P)]
            return g

        gL0 = cvt_layer(wL0, cL0, "L0")

        emit_ffn(em, st, dict(xT=xin, yT=xa, gates=gL0, **ffn_io(wL0, cL0)),
                 n_tok, tag="f0")
        s_at = cv.group("attn")
        cv.grouped(s_at, qkvc, qkvT_f, DM, 3 * DM, 512)
        t_at = cv.grouped(s_at, projc, projT_f, DM, DM, P)
        gat = {"qkv": [t_at], "proj": [t_at]}
        stepL1 = lazy_cvt(cv, wL1, cL1, cv.group("L1"))
        barrier()
        emit_attn0(em, st, dict(xT=xa, yT=xb, gates=gat, bg=stepL1, **at), n_tok)
        tL1 = stepL1(999)
        gL1 = {"g12": [tL1], "h1": [tL1], "h2": [tL1], "w3": [tL1]}
        barrier()
        emit_ffn(em, st, dict(xT=xb, yT=xa, **ffn_io(wL0, cL0)), n_tok, tag="f1")
        barrier()
        emit_ffn(em, st, dict(xT=xa, yT=xb, gates=gL1, **ffn_io(wL1, cL1)),
                 n_tok, tag="f2")
        barrier()
        emit_attn1(em, st, dict(xT=xb, yT=xa), n_tok)
        barrier()
        toks = emit_ffn(em, st, dict(xT=xa, yT=yout, **ffn_io(wL1, cL1)),
                        n_tok, tag="f3")
        em.wait("sync", toks)
        em.finish()
    return nc


def kernel(**inputs):
    x = np.asarray(inputs["L0_attn_x"])
    B, N, C = x.shape
    xT = [np.ascontiguousarray(x[b].T) for b in range(B)]

    nc = build_fused(N)
    wd = {}
    for L in ("L0", "L1"):
        wd[f"w1T_{L}"] = _t(inputs[f"{L}_ffn_w1_w"])
        wd[f"w2T_{L}"] = _t(inputs[f"{L}_ffn_w2_w"])
        wd[f"h1T_{L}"] = _t(inputs[f"{L}_ffn_h1_w"])
        wd[f"h2T_{L}"] = _t(inputs[f"{L}_ffn_h2_w"])
        wd[f"w3T_{L}"] = _t(inputs[f"{L}_ffn_w3_w"])
        wd[f"s1_{L}"] = np.asarray(inputs[f"{L}_ffn_w1_s"])
        wd[f"s2_{L}"] = np.asarray(inputs[f"{L}_ffn_w2_s"])
        wd[f"sh1_{L}"] = np.asarray(inputs[f"{L}_ffn_h1_s"])
        wd[f"sh2_{L}"] = np.asarray(inputs[f"{L}_ffn_h2_s"])
        wd[f"s3_{L}"] = np.asarray(inputs[f"{L}_ffn_w3_s"])
    wd["qkvT"] = _t(inputs["L0_attn_qkv_w"])
    wd["projT"] = _t(inputs["L0_attn_proj_w"])
    wd["pb"] = np.asarray(inputs["L0_attn_proj_b"])
    wd["t16"] = np.ascontiguousarray(np.asarray(inputs["L0_attn_temperature"]).ravel())
    wd["trep"] = np.ascontiguousarray(np.repeat(
        np.asarray(inputs["L0_attn_temperature"]).ravel(), 64))

    in_maps = [dict(x_in=xT[b], **wd) for b in range(B)]
    res = run_bass_kernel_spmd(nc, in_maps, core_ids=list(range(B)))
    out = np.stack([np.asarray(r["y_out"], dtype=np.float32).T for r in res.results])
    return out.astype(np.float32)


def emit_attn0(em, st, io, n_tok, tag="a0"):
    nc = em.nc
    xT, yT = io["xT"], io["yT"]
    qkvT, projT, pb, t16, trep = io["qkvT"], io["projT"], io["pb"], io["t16"], io["trep"]
    qsc, vsc = io["qsc"], io["vsc"]
    gates = io.get("gates") or {}
    g_qkv = gates.get("qkv", [])
    g_proj = gates.get("proj", [])
    bg = io.get("bg")
    NH = 16
    HD = 64
    nch = n_tok // TC
    njt = n_tok // P
    with ExitStack() as ctx:
        e_ = lambda cm: ctx.enter_context(cm)
        x_sb = e_(nc.sbuf_tensor(f"x_sb_{tag}", [P, NFC * TC], F32))
        y_bf = e_(nc.sbuf_tensor("y_bf", [P, NFC * TC], BF16))
        k_sb = e_(nc.sbuf_tensor("k_sb", [P, NFC * n_tok], BF16))
        q_bf = e_(nc.sbuf_tensor("q_bf", [P, NFC * TC], BF16))
        k_bf = e_(nc.sbuf_tensor("k_bf", [P, NFC * TC], BF16))
        v_bf = e_(nc.sbuf_tensor("v_bf", [P, NFC * TC], BF16))
        vbuild = e_(nc.sbuf_tensor("vbuild", [P, 4 * NH * 65], BF16))
        qkvb = [e_(nc.sbuf_tensor(f"qkvb{i}", [P, NFC * 512], BF16)) for i in range(2)]
        Tb = [e_(nc.sbuf_tensor(f"Tb{i}", [P, 2 * TC], BF16)) for i in range(2)]
        y_attn = e_(nc.sbuf_tensor("y_attn", [P, NFC * TC], BF16))
        zrow_sb = e_(nc.sbuf_tensor("zrow_sb", [1, 2 * TC], F32))
        zrb_sb = e_(nc.sbuf_tensor("zrb_sb", [64, 2 * TC], F32))
        projb = [e_(nc.sbuf_tensor(f"projb{i}", [P, NFC * P], BF16)) for i in range(2)]
        out_sb = e_(nc.sbuf_tensor(f"out_sb_{tag}", [P, 2 * TC], F32))
        rr = e_(nc.sbuf_tensor("rr", [1, 4 * TC], F32))
        sraw = e_(nc.sbuf_tensor("sraw", [NFC, P], F32))
        invtc = e_(nc.sbuf_tensor("invtc", [P, NFC], F32))
        pbc = e_(nc.sbuf_tensor("pbc", [P, NFC], F32))
        negc = e_(nc.sbuf_tensor("negc", [P, NH], F32))
        t16r = e_(nc.sbuf_tensor("t16r", [1, NH], F32))
        invt16 = e_(nc.sbuf_tensor("invt16", [1, NH], F32))
        abst = e_(nc.sbuf_tensor("abst", [1, NH], F32))
        oneq_row = e_(nc.sbuf_tensor("oneq_row", [1, P], F32))
        negones = e_(nc.sbuf_tensor("negones", [1, P], F32))

        qs_sb = e_(nc.sbuf_tensor("qs_sb", [P, NFC * TC], BF16))
        pQKV = [e_(nc.psum_tensor(f"pQKV{i}", [P, TC], F32)) for i in range(2)]
        ctx1 = ExitStack()
        pV = [ctx1.enter_context(nc.psum_tensor(f"pV{i}", [P, P], BF16))
              for i in range(2)]
        pQS = [ctx1.enter_context(nc.psum_tensor(f"pQS{i}", [P, TC], F32))
               for i in range(2)]

        st["m_pe"] = st["m_pe"]
        s_pool = st.sem("s_pool2")
        d_x = st.sem("d_x0")
        d_qkv = [st.sem("d_qkva"), st.sem("d_qkvb")]
        d_qs = st.sem("d_qs")
        d_vs = st.sem("d_vs")
        d_qc = [st.sem("d_qc0"), st.sem("d_qc1")]
        d_vh = [st.sem("d_vh0"), st.sem("d_vh1")]
        d_pj = [st.sem("d_pj0"), st.sem("d_pj1")]
        d_out = [st.sem("d_out0"), st.sem("d_out1")]
        d_misc = st.sem("d_misc")
        d_misc2 = st.sem("d_misc2")
        consts = {"tok": st["const_tok"]}
        id32 = st["id32"]

        id_bf = st["id_bf"]
        tib = st["id_bf_tok"]
        em.op("gpsimd", lambda e: e.memset(oneq_row[:], 0.125), inc=s_pool)
        em.op("gpsimd", lambda e: e.memset(negones[:], -1.0), inc=s_pool)
        pool_tok = (s_pool, s_pool.n)

        tl1 = em.op("sync", lambda e: e.dma_start(
            sraw[:, :], trep.rearrange("(c p) -> c p", p=P)), inc=d_misc, inck=16)
        ttr1 = em.op("tensor", lambda e: e.transpose(
            st["pBC0"][:, 0:NFC], sraw[:, :], id32[0:NFC, 0:NFC]),
            waits=[tl1, consts["tok"]], inc=st["m_pe"])
        tc1 = em.op("vector", lambda e: e.reciprocal(invtc[:, :], st["pBC0"][:, 0:NFC]),
                    waits=[ttr1], inc=st["m_dve"])
        tl2 = em.op("sync", lambda e: e.dma_start(
            sraw[:, :], pb.rearrange("(c p) -> c p", p=P)), waits=[tc1], inc=d_misc, inck=16)
        ttr2 = em.op("tensor", lambda e: e.transpose(
            st["pBC0"][:, 0:NFC], sraw[:, :], id32[0:NFC, 0:NFC]),
            waits=[tl2, tc1], inc=st["m_pe"])
        tc2 = em.op("vector", lambda e: e.tensor_copy(pbc[:, :], st["pBC0"][:, 0:NFC]),
                    waits=[ttr2], inc=st["m_dve"])
        tl3 = em.op("sync", lambda e: e.dma_start(t16r[0:1, :], t16[None, :]),
                    inc=d_misc2, inck=16)
        tc3 = em.op("vector", lambda e: e.reciprocal(invt16[0:1, :], t16r[0:1, :]),
                    waits=[tl3], inc=st["m_dve"])
        ta3 = em.op("scalar", lambda e: e.activation(
            abst[0:1, :], invt16[0:1, :], AF.Abs, scale=0.015625),
            waits=[tc3], inc=st["m_act"])
        tb3 = em.op("tensor", lambda e: e.matmul(
            st["pBC1"][:, 0:NH], negones[0:1, :], abst[0:1, :], start=True, stop=True),
            waits=[ta3, pool_tok], inc=st["m_pe"])
        tc4 = em.op("vector", lambda e: e.tensor_copy(negc[:, :], st["pBC1"][:, 0:NH]),
                    waits=[tb3], inc=st["m_dve"])
        setup_dve = tc4
        ln_bank_war = [tc2, tc4]

        tone = em.op("gpsimd", lambda e: e.memset(
            vbuild[:].rearrange("p (tt h d) -> p tt h d", tt=4, h=NH, d=65)
            [:, :, :, 64:65], 1.0), inc=s_pool)

        x_free = None
        y_war = None
        war_banks = ln_bank_war
        qs_store = None
        vb_free = None
        ksc_done = []
        for ci in range(nch):
            t_x = em.op("sync", lambda e, ci=ci: e.dma_start(
                x_sb[:].rearrange("p (c t) -> p c t", t=TC),
                xT[:, ci * TC:(ci + 1) * TC].rearrange("(c p) t -> p c t", p=P)),
                waits=[x_free], inc=d_x, inck=16)
            y_all, xreads = emit_ln(em, st, x_sb, t_x, y_bf,
                                    war_y=(y_war or []), war_banks=war_banks)
            x_free = xreads[-1]

            qkv_toks = [None] * 6
            for g in range(2):
                qkv_toks[g] = em.op("gpsimd", lambda e, g=g: e.dma_start(
                    qkvb[g % 2][:], qkvT[g, :, :]),
                    waits=[st[f"qkv_free{g % 2}"] if f"qkv_free{g % 2}" in st else None]
                    + (g_qkv if ci == 0 else []),
                    inc=d_qkv[g % 2], inck=16)
            pq_free = [None, None]
            q_evac = [None] * NFC
            k_evac = [None] * NFC
            v_evac = [None] * NFC
            qsq_mm = None
            ksq_mm = None
            sq_scr_free = [None, None]
            last_qkv_mm = [None, None]
            pend_sum = None

            def emit_pend(kind, mi, ms_, tsq_):
                tgt = pQS[0] if kind == "q" else pQS[1]
                return em.op("tensor", lambda e, mi=mi, ms_=ms_, tgt=tgt: e.matmul(
                    tgt[0:1, 0:TC], st["ones_col"][:],
                    st["xsq"][:, ms_ * TC:(ms_ + 1) * TC],
                    start=(mi == 0), stop=(mi == NFC - 1)),
                    waits=[tsq_] + (st.get("pQS_war", []) if mi == 0 else []),
                    inc=st["m_pe"])
            for m in range(24):
                g, gs, ms = m // 4, (m // 4) % 2, m % 2
                j = m % 4
                mm = None
                for k in range(NFC):
                    mm = em.op("tensor", lambda e, k=k, gs=gs, j=j, ms=ms: e.matmul(
                        pQKV[ms][:, :], qkvb[gs][:, k * 512 + j * P:k * 512 + (j + 1) * P],
                        y_bf[:, k * TC:(k + 1) * TC],
                        start=(k == 0), stop=(k == NFC - 1)),
                        waits=([qkv_toks[g], y_all, pq_free[ms]] if k == 0 else []),
                        inc=(st["m_pe"] if k == NFC - 1 else None))
                last_qkv_mm[gs] = mm
                if pend_sum is not None:
                    _t = emit_pend(*pend_sum)
                    sq_scr_free[pend_sum[2]] = _t
                    if pend_sum[0] == "q":
                        qsq_mm = _t
                    else:
                        ksq_mm = _t
                    pend_sum = None
                if j == 3 and g + 2 < 6:
                    qkv_toks[g + 2] = em.op("gpsimd", lambda e, g=g, gs=gs: e.dma_start(
                        qkvb[gs][:], qkvT[g + 2, :, :]),
                        waits=[mm], inc=d_qkv[gs], inck=16)
                if m < 8:
                    tq = em.op("scalar", lambda e, m=m, ms=ms: e.activation(
                        q_bf[:, m * TC:(m + 1) * TC], pQKV[ms][:, :], AF.Copy),
                        waits=[mm, q_evac[m - 2] if m >= 2 else None], inc=st["m_act"])
                    q_evac[m] = tq
                    tsq = em.op("scalar", lambda e, m=m, ms=ms: e.activation(
                        st["xsq"][:, ms * TC:(ms + 1) * TC], pQKV[ms][:, :], AF.Square),
                        waits=[mm, sq_scr_free[ms]], inc=st["m_act"])
                    pend_sum = ("q", m, ms, tsq)
                    pq_free[ms] = tsq
                elif m < 16:
                    mk = m - 8
                    tk = em.op("scalar", lambda e, mk=mk, ms=ms: e.activation(
                        k_bf[:, mk * TC:(mk + 1) * TC], pQKV[ms][:, :], AF.Copy),
                        waits=[mm, k_evac[mk - 2] if mk >= 2 else None], inc=st["m_act"])
                    k_evac[mk] = tk
                    tsq = em.op("scalar", lambda e, ms=ms: e.activation(
                        st["xsq"][:, ms * TC:(ms + 1) * TC], pQKV[ms][:, :], AF.Square),
                        waits=[mm, sq_scr_free[ms]], inc=st["m_act"])
                    pend_sum = ("k", mk, ms, tsq)
                    pq_free[ms] = tsq
                else:
                    mv = m - 16
                    tv = em.op("scalar", lambda e, mv=mv, ms=ms: e.activation(
                        v_bf[:, mv * TC:(mv + 1) * TC], pQKV[ms][:, :], AF.Copy),
                        waits=[mm, v_evac[mv - 2] if mv >= 2 else None,
                               st.get("vbf_war")], inc=st["m_act"])
                    v_evac[mv] = tv
                    pq_free[ms] = tv
            st["qkv_free0"] = last_qkv_mm[0]
            st["qkv_free1"] = last_qkv_mm[1]

            def rrow(i):
                return rr[0:1, i * TC:(i + 1) * TC]
            tq1 = em.op("scalar", lambda e: e.activation(
                rrow(0), pQS[0][0:1, 0:TC], AF.Ln,
                bias=st["eps24"][0:1, 0:1]),
                waits=[qsq_mm, st.get("rr_war")], inc=st["m_act"])
            tk1 = em.op("scalar", lambda e: e.activation(
                rrow(1), pQS[1][0:1, 0:TC], AF.Ln,
                bias=st["eps24"][0:1, 0:1]), waits=[ksq_mm], inc=st["m_act"])
            tq3 = em.op("scalar", lambda e: e.activation(
                rrow(2), rrow(0), AF.Exp, scale=-0.5),
                waits=[tq1] + st.get("rr_war2", []), inc=st["m_act"])
            tk3 = em.op("scalar", lambda e: e.activation(
                rrow(3), rrow(1), AF.Exp, scale=-0.5),
                waits=[tk1], inc=st["m_act"])
            st["rr_war"] = tk3
            tbq = em.op("tensor", lambda e: e.matmul(
                pQS[0][:, :], oneq_row[0:1, :], rrow(2), start=True, stop=True),
                waits=[tq3, pool_tok], inc=st["m_pe"])
            tbk = em.op("tensor", lambda e: e.matmul(
                pQS[1][:, :], oneq_row[0:1, :], rrow(3), start=True, stop=True),
                waits=[tk3], inc=st["m_pe"])
            st["rr_war2"] = [tbq, tbk]

            qs_last = None
            ks_last = None
            for c in range(NFC):
                tqs_ = em.op("vector", lambda e, c=c: e.tensor_tensor(
                    qs_sb[:, c * TC:(c + 1) * TC], q_bf[:, c * TC:(c + 1) * TC],
                    pQS[0][:, :], ALU.mult),
                    waits=[tbq, q_evac[c], qs_store],
                    inc=st["m_dve"])
                qs_last = tqs_
                tks_ = em.op("vector", lambda e, c=c, ci=ci: e.scalar_tensor_tensor(
                    k_sb[:, c * n_tok + ci * TC:c * n_tok + (ci + 1) * TC],
                    k_bf[:, c * TC:(c + 1) * TC], invtc[:, c:c + 1],
                    pQS[1][:, :], ALU.mult, ALU.mult),
                    waits=[tbk, k_evac[c], tc1, setup_dve], inc=st["m_dve"])
                ks_last = tks_
            qs_store = em.op("sync", lambda e, ci=ci: e.dma_start(
                qsc[:, ci * TC:(ci + 1) * TC].rearrange("(c p) t -> p c t", p=P),
                qs_sb[:].rearrange("p (c t) -> p c t", t=TC)),
                waits=[qs_last], inc=d_qs, inck=16)
            y_war = [last_qkv_mm[0], last_qkv_mm[1]]
            war_banks = [st["bank_free_BC"]]
            st["pQS_war"] = [qs_last, ks_last]
            ksc_done.append(ks_last)

            vcp_last = None
            for c in range(NFC):
                for tt in range(4):
                    vs_ = (c * 4 + tt) % 2
                    ttv = em.op("tensor", lambda e, c=c, tt=tt, vs_=vs_: e.transpose(
                        pV[vs_][:, :], v_bf[:, c * TC + tt * P:c * TC + (tt + 1) * P],
                        id_bf[:, :]),
                        waits=[v_evac[c], tib, st.get(f"pV_free{vs_}")], inc=st["m_pe"])
                    tcv = em.op("vector", lambda e, c=c, tt=tt, vs_=vs_: e.tensor_copy(
                        vbuild[:].rearrange("p (tt h d) -> p tt h d", tt=4, h=NH, d=65)
                        [:, tt, 2 * c:2 * c + 2, 0:64],
                        pV[vs_][:].rearrange("p (h d) -> p h d", h=2, d=64)),
                        waits=[ttv, vb_free], inc=st["m_dve"])
                    st[f"pV_free{vs_}"] = tcv
                    vcp_last = tcv
            vs_store = None
            for h in range(NH):
                vs_store = em.op("sync", lambda e, ci=ci, h=h: e.dma_start(
                    vsc[h, ci * TC:(ci + 1) * TC, :].rearrange("(tt p) c -> p tt c", p=P),
                    vbuild[:].rearrange("p (tt h c) -> p tt h c", tt=4, h=NH, c=65)[:, :, h, :]),
                    waits=[vcp_last, tone] if h == 0 else [], inc=d_vs, inck=16)
            vb_free = vs_store
            st["vbf_war"] = vcp_last

        vhead = [vbuild[:, 0:njt * 65], vbuild[:, njt * 65:2 * njt * 65]]
        qch = [v_bf, y_bf]
        p1_done = [(st["m_pe"], st["m_pe"].n), (st["m_act"], st["m_act"].n),
                   (st["m_dve"], st["m_dve"].n), (s_pool, s_pool.n)]
        ctx1.close()
        pS = [e_(nc.psum_tensor(f"pS{i}", [P, 2 * TC], F32)) for i in range(2)]
        k_all = ksc_done[-1]
        NP = NH * njt // 2
        qc_free = [None, None]
        vh_free = [None, None]
        pS_carry = [None, None]
        Tb_free = [None, None]
        av_start = [[], []]
        pj_free = [None, None]
        out_free = [None, None]
        zrow_free = [None, None]
        zrb_free = [None, None]
        ya_war = None
        pzr_free = None
        ppj_free = [None, None]
        res_last = None

        def sc_mm(s, ics, waits):
            h, jt = s // njt, s % njt
            cfc, base = h // 2, (h % 2) * 64
            ps, half = (s // 2) % 2, s % 2
            return em.op("tensor", lambda e, h=h, jt=jt, cfc=cfc, base=base,
                         ps=ps, half=half, ics=ics: e.matmul(
                pS[ps][:, half * TC:(half + 1) * TC],
                k_sb[base:base + 64, cfc * n_tok + jt * P:cfc * n_tok + (jt + 1) * P],
                qch[ics][base:base + 64, cfc * TC:(cfc + 1) * TC],
                start=True, stop=True),
                waits=waits, inc=(st["m_pe"] if half == 1 else None))

        for ic in range(nch):
            ics = ic % 2
            t_q = em.op("sync", lambda e, ic=ic, ics=ics: e.dma_start(
                qch[ics][:, :].rearrange("p (c t) -> p c t", t=TC),
                qsc[:, ic * TC:(ic + 1) * TC].rearrange("(c p) t -> p c t", p=P)),
                waits=[qc_free[ics], (d_qs, d_qs.n)] + (p1_done if ic < 2 else []),
                inc=d_qc[ics], inck=16)
            t_x2 = em.op("sync", lambda e, ic=ic: e.dma_start(
                x_sb[:].rearrange("p (c t) -> p c t", t=TC),
                xT[:, ic * TC:(ic + 1) * TC].rearrange("(c p) t -> p c t", p=P)),
                waits=[x_free], inc=d_x, inck=16)
            vh_toks = [None] * NH
            for i in range(2):
                vh_toks[i] = em.op("gpsimd", lambda e, i=i: e.dma_start(
                    vhead[i].rearrange("p (jt c) -> p jt c", c=65),
                    vsc[i, :, :].rearrange("(jt p) c -> p jt c", p=P)),
                    waits=[vh_free[i], (d_vs, d_vs.n)] + (p1_done if ic == 0 else []),
                    inc=d_vh[i], inck=16)

            sc_mm(0, ics, [k_all, t_q, pS_carry[0]])
            sc1 = {0: sc_mm(1, ics, [])}
            exp_toks = {}
            ya_last = None
            ep_args = None

            def emit_ep(args):
                nonlocal pzr_free, ya_last, ya_war
                h0, hs0, cfc0, base0, tv0 = args
                tZc = em.op("scalar", lambda e, hs0=hs0: e.activation(
                    zrow_sb[0:1, hs0 * TC:(hs0 + 1) * TC], pQKV[hs0][64:65, :],
                    AF.Copy), waits=[tv0, zrow_free[hs0]], inc=st["m_act"])
                tbz = em.op("tensor", lambda e, hs0=hs0: e.matmul(
                    st["pBC0"][0:64, :], st["ones_row"][0:1, 0:64],
                    zrow_sb[0:1, hs0 * TC:(hs0 + 1) * TC],
                    start=True, stop=True), waits=[tZc, pzr_free], inc=st["m_pe"])
                trc = em.op("vector", lambda e, hs0=hs0: e.reciprocal(
                    zrb_sb[0:64, hs0 * TC:(hs0 + 1) * TC], st["pBC0"][0:64, :]),
                    waits=[tbz, zrb_free[hs0]], inc=st["m_dve"])
                tya = em.op("vector", lambda e, hs0=hs0, cfc0=cfc0, base0=base0:
                            e.tensor_tensor(
                    y_attn[base0:base0 + 64, cfc0 * TC:(cfc0 + 1) * TC],
                    pQKV[hs0][0:64, :], zrb_sb[0:64, hs0 * TC:(hs0 + 1) * TC],
                    ALU.mult), waits=[trc, ya_war], inc=st["m_dve"])
                av_start[hs0] = [tya]
                pzr_free = trc
                zrow_free[hs0] = tbz
                zrb_free[hs0] = tya
                ya_last = tya

            for p in range(NP):
                s0 = 2 * p
                h, jt0 = s0 // njt, s0 % njt
                hs, ts = h % 2, p % 2
                cfc, base = h // 2, (h % 2) * 64
                if p + 1 < NP:
                    w0 = [exp_toks[p - 1]] if p >= 1 else [k_all, t_q, pS_carry[1]]
                    sc_mm(s0 + 2, ics, w0)
                    sc1[p + 1] = sc_mm(s0 + 3, ics, [])
                tex = em.op("scalar", lambda e, ps=p % 2, ts=ts, h=h: e.activation(
                    Tb[ts][:, :], pS[ps][:, :], AF.Exp, bias=negc[:, h:h + 1]),
                    waits=[sc1[p], tc4, Tb_free[ts]], inc=st["m_act"])
                exp_toks[p] = tex
                em.op("tensor", lambda e, hs=hs, jt0=jt0, ts=ts: e.matmul(
                    pQKV[hs][0:65, :], vhead[hs][:, jt0 * 65:(jt0 + 1) * 65],
                    Tb[ts][:, 0:TC],
                    start=(jt0 == 0), stop=False, skip_group_check=True),
                    waits=[tex, vh_toks[h]] + (av_start[hs] if jt0 == 0 else []))
                t_av1 = em.op("tensor", lambda e, hs=hs, jt0=jt0, ts=ts: e.matmul(
                    pQKV[hs][0:65, :], vhead[hs][:, (jt0 + 1) * 65:(jt0 + 2) * 65],
                    Tb[ts][:, TC:2 * TC],
                    start=False, stop=(jt0 + 1 == njt - 1), skip_group_check=True),
                    waits=[], inc=st["m_pe"])
                Tb_free[ts] = t_av1
                if ep_args is not None:
                    emit_ep(ep_args)
                    ep_args = None
                if jt0 + 1 == njt - 1:
                    if h + 2 < NH:
                        vh_toks[h + 2] = em.op("gpsimd", lambda e, h=h, hs=hs: e.dma_start(
                            vhead[hs].rearrange("p (jt c) -> p jt c", c=65),
                            vsc[h + 2, :, :].rearrange("(jt p) c -> p jt c", p=P)),
                            waits=[t_av1], inc=d_vh[hs], inck=16)
                    vh_free[hs] = t_av1
                    ep_args = (h, hs, cfc, base, t_av1)
                    if bg is not None:
                        bg(1)
            emit_ep(ep_args)
            ep_args = None
            pS_carry = [exp_toks[NP - 2], exp_toks[NP - 1]]
            st["bank_free_BC"] = ya_last

            pj_toks = [None] * NFC
            for g in range(2):
                pj_toks[g] = em.op("gpsimd", lambda e, g=g: e.dma_start(
                    projb[g][:], projT[g, :, :]),
                    waits=[ppj_free[g]] + (g_proj if ic == 0 else []),
                    inc=d_pj[g], inck=16)
            for m in range(NFC):
                ws = m % 2
                t1 = None
                for k in range(NFC):
                    t1 = em.op("tensor", lambda e, k=k, ws=ws: e.matmul(
                        pQKV[ws][:, :],
                        projb[ws][:, k * P:(k + 1) * P],
                        y_attn[:, k * TC:(k + 1) * TC],
                        start=(k == 0), stop=(k == NFC - 1), skip_group_check=True),
                        waits=([pj_toks[m], ya_last, pj_free[ws]] +
                               (av_start[ws] if m < 2 else []) if k == 0 else []),
                        inc=(st["m_pe"] if k == NFC - 1 else None))
                ppj_free[ws] = t1
                if m + 2 < NFC:
                    pj_toks[m + 2] = em.op("gpsimd", lambda e, m=m, ws=ws: e.dma_start(
                        projb[ws][:], projT[m + 2, :, :]),
                        waits=[t1], inc=d_pj[ws], inck=16)
                tr_ = em.op("vector", lambda e, m=m, ws=ws: e.scalar_tensor_tensor(
                    out_sb[:, ws * TC:(ws + 1) * TC],
                    pQKV[ws][:, :],
                    pbc[:, m:m + 1], x_sb[:, m * TC:(m + 1) * TC],
                    ALU.add, ALU.add),
                    waits=[t1, tc2, t_x2, out_free[ws]], inc=st["m_dve"])
                pj_free[ws] = tr_
                ts_ = em.op("sync", lambda e, m=m, ws=ws, ic=ic: e.dma_start(
                    yT[m * P:(m + 1) * P, ic * TC:(ic + 1) * TC],
                    out_sb[:, ws * TC:(ws + 1) * TC]),
                    waits=[tr_], inc=d_out[ws], inck=16)
                out_free[ws] = ts_
                res_last = tr_
            av_start = [[ppj_free[0], pj_free[0]], [ppj_free[1], pj_free[1]]]
            qc_free[ics] = st["m_pe"], st["m_pe"].n
            x_free = res_last
            ya_war = ppj_free[1]

    return [(d_out[0], d_out[0].n), (d_out[1], d_out[1].n)]


def emit_attn1(em, st, io, n_tok, tag="a1"):
    nc = em.nc
    xT, yT = io["xT"], io["yT"]
    njt = n_tok // P
    nch = n_tok // TC
    IC2 = 256
    nic2 = n_tok // IC2
    with ExitStack() as ctx:
        e_ = lambda cm: ctx.enter_context(cm)
        x_sb = e_(nc.sbuf_tensor(f"x_sb_{tag}", [P, NFC * TC], F32))
        yln = e_(nc.sbuf_tensor(f"yln_{tag}", [P, NFC * n_tok], BF16))
        yT_sb = e_(nc.sbuf_tensor(f"yT_sb_{tag}", [P, njt * DM], BF16))
        rows2 = e_(nc.sbuf_tensor(f"rows2_{tag}", [1, 2 * TC], F32))
        nsr = e_(nc.sbuf_tensor(f"nsr_{tag}", [1, n_tok], F32))
        dump = e_(nc.sbuf_tensor(f"dump_{tag}", [P, 2 * TC], BF16))
        zp = e_(nc.sbuf_tensor(f"zp_{tag}", [P, 2], F32))
        Zc = e_(nc.sbuf_tensor(f"Zc_{tag}", [P, njt], F32))
        lnz = e_(nc.sbuf_tensor(f"lnz_{tag}", [P, njt], F32))
        negb = e_(nc.sbuf_tensor(f"negb_{tag}", [P, njt], F32))
        negM2c = e_(nc.sbuf_tensor(f"negM2c_{tag}", [P, 1], F32))
        m2row = e_(nc.sbuf_tensor(f"m2row_{tag}", [1, 2], F32))
        Pb = [e_(nc.sbuf_tensor(f"Pb{i}_{tag}", [P, IC2], BF16)) for i in range(2)]
        out_sb = e_(nc.sbuf_tensor(f"out_sb_{tag}", [P, 2 * IC2], F32))

        pc = [e_(nc.psum_tensor(f"pc{i}_{tag}", [P, TC], F32)) for i in range(4)]
        pTr = [e_(nc.psum_tensor(f"pTr{i}_{tag}", [P, P], BF16)) for i in range(2)]

        d_x = st.sem("d_x0")
        d_out = [st.sem("d_out0"), st.sem("d_out1")]
        consts = {"tok": st["const_tok"]}
        id_bf = st["id_bf"]
        tib = st["id_bf_tok"]

        x_free = None
        war_banks = None
        nsr_toks = []
        tr_cp_last = None
        ln_last_y = None
        for ci in range(nch):
            t_x = em.op("sync", lambda e, ci=ci: e.dma_start(
                x_sb[:].rearrange("p (c t) -> p c t", t=TC),
                xT[:, ci * TC:(ci + 1) * TC].rearrange("(c p) t -> p c t", p=P)),
                waits=[x_free], inc=d_x, inck=16)

            def nsr_cb(t_var, t_rstd, row, ci=ci):
                tu = em.op("vector", lambda e: e.tensor_tensor(
                    rows2[0:1, 0:TC], row(2), row(4), ALU.mult),
                    waits=[t_var, t_rstd], inc=st["m_dve"])
                tn = em.op("vector", lambda e, ci=ci: e.scalar_tensor_tensor(
                    nsr[0:1, ci * TC:(ci + 1) * TC], rows2[0:1, 0:TC], float(DM),
                    row(4), ALU.mult, ALU.mult),
                    waits=[tu], inc=st["m_dve"])
                return tn

            y_all, xreads = emit_ln(
                em, st, x_sb, t_x, None, war_banks=war_banks, stats_cb=nsr_cb,
                y_dest=lambda c, ci=ci: yln[:, c * n_tok + ci * TC:c * n_tok + (ci + 1) * TC])
            war_banks = [st["bank_free_BC"]]
            x_free = xreads[-1]
            ln_last_y = y_all
            for c in range(NFC):
                for tt in range(4):
                    jt = ci * 4 + tt
                    ts_ = (c * 4 + tt) % 2
                    ttv = em.op("tensor", lambda e, c=c, ci=ci, tt=tt, ts_=ts_: e.transpose(
                        pTr[ts_][:, :],
                        yln[:, c * n_tok + ci * TC + tt * P:c * n_tok + ci * TC + (tt + 1) * P],
                        id_bf[:, :]),
                        waits=[y_all, tib, st.get(f"pTr_free{ts_}")], inc=st["m_pe"])
                    tcv = em.op("vector", lambda e, c=c, jt=jt, ts_=ts_: e.tensor_copy(
                        yT_sb[:, jt * DM + c * P:jt * DM + (c + 1) * P], pTr[ts_][:, :]),
                        waits=[ttv], inc=st["m_dve"])
                    st[f"pTr_free{ts_}"] = tcv
                    tr_cp_last = tcv

        tm1 = em.op("vector", lambda e: e.reduce_max(
            m2row[0:1, 0:1], nsr[0:1, :], axis=AX.X),
            waits=nsr_toks + [nsr_toks[-1] if nsr_toks else None,
                              st["rows_free"]], inc=st["m_dve"])
        tm2 = em.op("scalar", lambda e: e.activation(
            m2row[0:1, 1:2], m2row[0:1, 0:1], AF.Copy, scale=-1.0),
            waits=[tm1], inc=st["m_act"])
        tm3 = em.op("tensor", lambda e: e.matmul(
            st["pBC0"][:, 0:1], st["ones_row"][0:1, :], m2row[0:1, 1:2],
            start=True, stop=True),
            waits=[tm2, st["bank_free_BC"], consts["tok"]], inc=st["m_pe"])
        tm4 = em.op("vector", lambda e: e.tensor_copy(negM2c[:, :], st["pBC0"][:, 0:1]),
                    waits=[tm3], inc=st["m_dve"])

        pS_free = [tm4, None]
        dump_free = [None, None]
        zl = [None] * njt
        zp_free = [None, None]
        for jt in range(njt):
            for ic in range(nch):
                s = ic % 2
                mm = None
                for c in range(NFC):
                    mm = em.op("tensor", lambda e, c=c, jt=jt, ic=ic, s=s: e.matmul(
                        (st["pBC0"] if s == 0 else st["pBC1"])[:, :],
                        yln[:, c * n_tok + jt * P:c * n_tok + (jt + 1) * P],
                        yln[:, c * n_tok + ic * TC:c * n_tok + (ic + 1) * TC],
                        start=(c == 0), stop=(c == NFC - 1)),
                        waits=([ln_last_y, pS_free[s]] if c == 0 else []),
                        inc=(st["m_pe"] if c == NFC - 1 else None))
                tex = em.op("scalar", lambda e, s=s: e.activation(
                    dump[:, s * TC:(s + 1) * TC], (st["pBC0"] if s == 0 else st["pBC1"])[:, :],
                    AF.Exp, bias=negM2c[:, 0:1], accum_out=zp[:, s:s + 1]),
                    waits=[mm, tm4, dump_free[s], zp_free[s]], inc=st["m_act"])
                pS_free[s] = tex
                dump_free[s] = tex
                if ic == 0:
                    tz = em.op("vector", lambda e, jt=jt, s=s: e.tensor_copy(
                        Zc[:, jt:jt + 1], zp[:, s:s + 1]), waits=[tex, zl[jt]],
                        inc=st["m_dve"])
                else:
                    tz = em.op("vector", lambda e, jt=jt, s=s: e.tensor_tensor(
                        Zc[:, jt:jt + 1], Zc[:, jt:jt + 1], zp[:, s:s + 1], ALU.add),
                        waits=[tex, zl[jt]], inc=st["m_dve"])
                zl[jt] = tz
                zp_free[s] = tz
        z_all = zl[-1]
        tl1 = em.op("scalar", lambda e: e.activation(
            lnz[:, :], Zc[:, :], AF.Ln), waits=[z_all], inc=st["m_act"])
        tl2 = em.op("vector", lambda e: e.tensor_scalar(
            negb[:, :], lnz[:, :], -1.0, None, ALU.mult),
            waits=[tl1], inc=st["m_dve"])
        tl3 = em.op("vector", lambda e: e.tensor_scalar(
            negb[:, :], negb[:, :], negM2c[:, 0:1], None, ALU.add),
            waits=[tl2], inc=st["m_dve"])

        pS_free = [pS_free[0], pS_free[1]]
        Pb_free = [None, None]
        pc_free = None
        out_free = [None, None]
        res_last = None
        def b1_mm(jt, ic2, waits):
            s = jt % 2
            t = None
            for c in range(NFC):
                t = em.op("tensor", lambda e, c=c, jt=jt, ic2=ic2, s=s: e.matmul(
                    (st["pBC0"] if s == 0 else st["pBC1"])[:, 0:IC2],
                    yln[:, c * n_tok + jt * P:c * n_tok + (jt + 1) * P],
                    yln[:, c * n_tok + ic2 * IC2:c * n_tok + (ic2 + 1) * IC2],
                    start=(c == 0), stop=(c == NFC - 1)),
                    waits=(waits if c == 0 else []),
                    inc=(st["m_pe"] if c == NFC - 1 else None))
            return t

        for ic2 in range(nic2):
            t_x2 = em.op("sync", lambda e, ic2=ic2: e.dma_start(
                x_sb[:].rearrange("p (c t) -> p c t", t=IC2)[:, 0:NFC, :],
                xT[:, ic2 * IC2:(ic2 + 1) * IC2].rearrange("(c p) t -> p c t", p=P)),
                waits=[x_free], inc=d_x, inck=16)
            ctx_mm = None
            mmk = {0: b1_mm(0, ic2, [pS_free[0]])}
            for jt in range(njt):
                s = jt % 2
                if jt + 1 < njt:
                    mmk[jt + 1] = b1_mm(jt + 1, ic2, [pS_free[(jt + 1) % 2]])
                tex = em.op("scalar", lambda e, jt=jt, s=s: e.activation(
                    Pb[s][:, :], (st["pBC0"] if s == 0 else st["pBC1"])[:, 0:IC2],
                    AF.Exp, bias=negb[:, jt:jt + 1]),
                    waits=[mmk[jt], tl3, Pb_free[s]], inc=st["m_act"])
                pS_free[s] = tex
                for d in range(NFC):
                    bank, half = d // 2, d % 2
                    ctx_mm = em.op("tensor", lambda e, d=d, jt=jt, s=s, bank=bank, half=half: e.matmul(
                        pc[bank][:, half * IC2:(half + 1) * IC2],
                        yT_sb[:, jt * DM + d * P:jt * DM + (d + 1) * P],
                        Pb[s][:, :],
                        start=(jt == 0 and half == 0), stop=(jt == njt - 1 and half == 1),
                        skip_group_check=True),
                        waits=([tex, tr_cp_last] + ([pc_free] if jt == 0 and d == 0 else [])),
                        inc=(st["m_pe"] if (jt == njt - 1 and d == NFC - 1) else None))
                Pb_free[s] = ctx_mm
            ev_last = None
            for d in range(NFC):
                bank, half = d // 2, d % 2
                ws = d % 2
                tev = em.op("vector", lambda e, d=d, bank=bank, half=half, ws=ws: e.tensor_tensor(
                    out_sb[:, ws * IC2:(ws + 1) * IC2],
                    pc[bank][:, half * IC2:(half + 1) * IC2],
                    x_sb[:, d * IC2:(d + 1) * IC2], ALU.add),
                    waits=[ctx_mm, t_x2, out_free[ws]], inc=st["m_dve"])
                ev_last = tev
                ts_ = em.op("sync", lambda e, d=d, ws=ws, ic2=ic2: e.dma_start(
                    yT[d * P:(d + 1) * P, ic2 * IC2:(ic2 + 1) * IC2],
                    out_sb[:, ws * IC2:(ws + 1) * IC2]),
                    waits=[tev], inc=d_out[ws], inck=16)
                out_free[ws] = ts_
            pc_free = ev_last
            x_free = ev_last
            res_last = ev_last

    return [(d_out[0], d_out[0].n), (d_out[1], d_out[1].n)]
